# version 12
# baseline (speedup 1.0000x reference)
import numpy as np
from contextlib import ExitStack

import concourse.bass as bass
import concourse.mybir as mybir
from concourse.bass_utils import run_bass_kernel_spmd

F32 = mybir.dt.float32
BF16 = mybir.dt.bfloat16
AF = mybir.ActivationFunctionType
ALU = mybir.AluOpType

NCORES = 8
RG = [[0, 1, 2, 3, 4, 5, 6, 7]]
B, S, D = 4, 4096, 2048
SC = S // NCORES
NH, HN = 16, 128
DI = 8192
NF = D // 128
NU = DI // 128
NG = B * NH
NKT = S // 128
SCALE_QK = float(1.0 / np.sqrt(np.float64(HN)))
ISD = float(1.0 / np.sqrt(np.float64(D)))
ISI = float(1.0 / np.sqrt(np.float64(DI)))
EPS = 1e-5

ENGINES = ("sync", "vector", "scalar", "tensor", "gpsimd")


def _hadamard(n):
    h = np.ones((1, 1), np.float32)
    while h.shape[0] < n:
        h = np.block([[h, h], [h, -h]])
    return h


class Sched:

    def __init__(self):
        self.ops = {e: [] for e in ENGINES}
        self.cnt = {k: 0 for k in (
            "qs", "qa", "qp", "cc", "pe", "dv", "ac",
            "qk0", "qk1", "qv0", "qv1", "qw0", "qw1", "qu0", "qu1", "qu2",
            "qg0", "qg1", "qg2", "qd0", "qd1", "qd2", "qd3", "qx0", "qx1",
            "qpr0", "qpr1", "qks0", "qks1", "qvs0", "qvs1")}

    def emit(self, eng, meth, *args, inc=None, inc_n=1, **kw):
        if inc is not None:
            self.cnt[inc] += inc_n
        self.ops[eng].append(("c", meth, args, kw, inc, inc_n))
        return self.cnt[inc] if inc is not None else None

    def dma(self, eng, out, in_, inc, **kw):
        return self.emit(eng, "dma_start", inc=inc, inc_n=16, out=out, in_=in_,
                         **kw)

    def wait(self, eng, sem, val):
        if val is None or val <= 0:
            return
        self.ops[eng].append(("w", sem, val))

    def check(self):
        for e in ENGINES:
            for op in self.ops[e]:
                if op[0] == "w":
                    assert op[2] <= self.cnt[op[1]], (
                        f"{e} waits {op[1]}>={op[2]} final {self.cnt[op[1]]}")

    def replay(self, name, eng, sems):
        last = {}
        for op in self.ops[name]:
            if op[0] == "w":
                _, sem, v = op
                if last.get(sem, -1) >= v:
                    continue
                last[sem] = v
                eng.wait_ge(sems[sem], v)
            else:
                _, meth, args, kw, inc, inc_n = op
                inst = getattr(eng, meth)(*args, **kw)
                if inc is not None:
                    inst.then_inc(sems[inc], inc_n)


def build(limit=None, dump=False):
    nc = bass.Bass("TRN2", target_bir_lowering=False)

    x_loc = nc.dram_tensor("x_loc", [B, SC, D], F32, kind="ExternalInput").ap()
    wq_sl = nc.dram_tensor("wq_sl", [768, D], F32, kind="ExternalInput").ap()
    wp_sl = nc.dram_tensor("wp_sl", [256, D], F32, kind="ExternalInput").ap()
    up_sl = nc.dram_tensor("up_sl", [1024, D], F32, kind="ExternalInput").ap()
    gt_sl = nc.dram_tensor("gt_sl", [1024, D], F32, kind="ExternalInput").ap()
    dn_sl = nc.dram_tensor("dn_sl", [256, DI], F32, kind="ExternalInput").ap()
    v_suug = nc.dram_tensor("v_suug", [1, D], F32, kind="ExternalInput").ap()
    v_sudp = nc.dram_tensor("v_sudp", [1, DI], F32, kind="ExternalInput").ap()
    v_wsug = nc.dram_tensor("v_wsug", [1, 2 * DI], F32, kind="ExternalInput").ap()
    v_wsdp = nc.dram_tensor("v_wsdp", [1, D], F32, kind="ExternalInput").ap()
    v_bpr = nc.dram_tensor("v_bpr", [1, D], F32, kind="ExternalInput").ap()
    c_h128d = nc.dram_tensor("c_h128", [128, 128], BF16, kind="ExternalInput").ap()
    c_idbd = nc.dram_tensor("c_idb", [128, 128], BF16, kind="ExternalInput").ap()
    c_idfd = nc.dram_tensor("c_idf", [128, 128], F32, kind="ExternalInput").ap()
    c_onesd = nc.dram_tensor("c_onesb", [128, 128], BF16, kind="ExternalInput").ap()
    out_loc = nc.dram_tensor("out_loc", [B, SC, D], F32, kind="ExternalOutput").ap()

    sh_attn = nc.dram_tensor("sh_attn", [1024, D], BF16, kind="Internal").ap()
    g_attn = nc.dram_tensor("g_attn", [8192, D], BF16, kind="Internal").ap()
    sh_ffn = nc.dram_tensor("sh_ffn", [3072, D], BF16, kind="Internal").ap()
    g_ffn = nc.dram_tensor("g_ffn", [24576, D], BF16, kind="Internal").ap()
    kv_sh = nc.dram_tensor("kv_sh", [8192, 1040], BF16, kind="Internal").ap()
    g_kv_lo = nc.dram_tensor("g_kv_lo", [32768, 1040], BF16, kind="Internal").ap()
    g_kv_hi = nc.dram_tensor("g_kv_hi", [32768, 1040], BF16, kind="Internal").ap()
    if dump:
        d_big1 = nc.dram_tensor("d_big1", [128, 65536], BF16,
                                kind="ExternalOutput").ap()
        d_big2 = nc.dram_tensor("d_big2", [128, 8192], BF16,
                                kind="ExternalOutput").ap()
        d_gattn = nc.dram_tensor("d_gattn", [8192, D], BF16,
                                 kind="ExternalOutput").ap()
        d_kvsh = nc.dram_tensor("d_kvsh", [8192, 1040], BF16,
                                kind="ExternalOutput").ap()
        d_gffn = nc.dram_tensor("d_gffn", [24576, D], BF16,
                                kind="ExternalOutput").ap()

    k_lo = g_kv_lo[:, 0:512].rearrange("(r g d) s -> g d r s", r=8, g=32)
    v_lo = g_kv_lo[:, 512:1040].rearrange("(r g k) c -> g k r c", r=8, g=32)
    k_hi = g_kv_hi[:, 0:512].rearrange("(r g d) s -> g d r s", r=8, g=32)
    v_hi = g_kv_hi[:, 512:1040].rearrange("(r g k) c -> g k r c", r=8, g=32)

    def k_view(g):
        return k_lo[g] if g < 32 else k_hi[g - 32]

    def v_view(g):
        return v_lo[g] if g < 32 else v_hi[g - 32]

    with ExitStack() as ctx:
        sbt = lambda name, shape, dt: ctx.enter_context(nc.sbuf_tensor(name, shape, dt))
        pst = lambda name, shape, dt: ctx.enter_context(nc.psum_tensor(name, shape, dt))

        c_h128 = sbt("c_h128s", [128, 128], BF16)
        c_idb = sbt("c_idbs", [128, 128], BF16)
        c_idf = sbt("c_idfs", [128, 128], F32)
        c_ones = sbt("c_oness", [128, 128], BF16)
        vecs = sbt("vecs", [1, 512], F32)
        col_su = sbt("col_su", [128, 16], F32)
        col_bp = sbt("col_bp", [128, 16], F32)
        col_wsu = sbt("col_wsu", [128, 64], F32)
        col_wsg = sbt("col_wsg", [128, 64], F32)
        col_wsd = sbt("col_wsd", [128, 16], F32)
        col_sdp = sbt("col_sdp", [128, 64], F32)
        row_a = sbt("row_a", [1, 512], F32)
        row_b = sbt("row_b", [1, 512], F32)
        row_bf = sbt("row_bf", [1, 1024], BF16)
        eps1 = sbt("eps1", [1, 1], F32)
        mb = sbt("mb", [128, 512], BF16)
        rb = sbt("rb", [128, 512], BF16)
        bca = sbt("bca", [128, 512], F32)
        sq = sbt("sq", [128, 2048], BF16)
        sgb = sbt("sgb", [128, 1024], BF16)
        kstg = sbt("kstg", [128, 1024], BF16)
        vtmp = sbt("vtmp", [128, 1024], BF16)
        vstg = sbt("vstg", [128, 1056], BF16)
        stg_out = sbt("stg_out", [128, 2048], BF16)
        xbf = sbt("xbf", [128, 4096], BF16)
        strips = sbt("strips", [128, 12288], BF16)
        big1 = sbt("big1", [128, 65536], BF16)
        big2 = sbt("big2", [128, 8192], BF16)

        pf = [pst(f"pf{i}", [128, 512], F32) for i in range(6)]
        pt = [pst(f"pt{i}", [128, 1024], BF16) for i in range(2)]

        sems = {}
        for sname in ("qs", "qa", "qp", "cc", "pe", "dv", "ac",
                      "qk0", "qk1", "qv0", "qv1", "qw0", "qw1",
                      "qu0", "qu1", "qu2", "qg0", "qg1", "qg2",
                      "qd0", "qd1", "qd2", "qd3",
                      "qx0", "qx1", "qpr0", "qpr1",
                      "qks0", "qks1", "qvs0", "qvs1"):
            sems[sname] = ctx.enter_context(nc.semaphore(sname))

        def strip_slot(i):
            return strips[:, i * 2048:(i + 1) * 2048]

        def U(g):
            return big1[:, g * 512:(g + 1) * 512]

        HB = 32768

        def xln(fb, tb=None):
            a = big1[:, HB + fb * 2048: HB + (fb + 1) * 2048]
            return a if tb is None else a[:, tb * 512:(tb + 1) * 512]

        r1 = xln

        def KB(p):
            return big1[:, HB + p * 4096: HB + (p + 1) * 4096]

        def VB(p):
            return big1[:, HB + 8192 + p * 4224: HB + 8192 + (p + 1) * 4224]

        def PT4(i):
            base = HB + 16640
            return big1[:, base + i * 512: base + (i + 1) * 512]

        def big2c(cidx, tt=None):
            a = big2[:, cidx * 512:(cidx + 1) * 512]
            return a if tt is None else a[:, tt * 128:(tt + 1) * 128]

        sch = Sched()
        C = sch.cnt

        def _phase_end(name):
            if limit == name:
                sch.wait("sync", "qs", C["qs"])
                sch.wait("sync", "qa", C["qa"])
                sch.wait("sync", "qp", C["qp"])
                sch.wait("sync", "qk0", C["qk0"])
                sch.wait("sync", "qk1", C["qk1"])
                sch.wait("sync", "qv0", C["qv0"])
                sch.wait("sync", "qv1", C["qv1"])
                sch.wait("sync", "qw0", C["qw0"])
                sch.wait("sync", "qw1", C["qw1"])
                sch.wait("sync", "qu0", C["qu0"])
                sch.wait("sync", "qu1", C["qu1"])
                sch.wait("sync", "qu2", C["qu2"])
                sch.wait("sync", "qg2", C["qg2"])
                sch.wait("sync", "qg0", C["qg0"])
                sch.wait("sync", "qg1", C["qg1"])
                sch.wait("sync", "qd0", C["qd0"])
                sch.wait("sync", "qd1", C["qd1"])
                sch.wait("sync", "qd2", C["qd2"])
                sch.wait("sync", "qd3", C["qd3"])
                sch.wait("sync", "qx0", C["qx0"])
                sch.wait("sync", "qx1", C["qx1"])
                sch.wait("sync", "qpr0", C["qpr0"])
                sch.wait("sync", "qpr1", C["qpr1"])
                sch.wait("sync", "qks0", C["qks0"])
                sch.wait("sync", "qks1", C["qks1"])
                sch.wait("sync", "qvs0", C["qvs0"])
                sch.wait("sync", "qvs1", C["qvs1"])
                sch.wait("gpsimd", "qp", C["qp"])
                return True
            return False

        sch.dma("sync", c_h128[:], c_h128d[:, :], "qs")
        sch.dma("sync", c_idb[:], c_idbd[:, :], "qs")
        sch.dma("sync", c_idf[:], c_idfd[:, :], "qs")
        sch.dma("sync", c_ones[:], c_onesd[:, :], "qs")
        c0_qs = C["qs"]
        sch.emit("vector", "memset", vstg[:], 1.0, inc="dv")
        sch.emit("vector", "memset", eps1[:], EPS, inc="dv")

        colplan = [
            (v_suug, [(col_su, 16, 1.0)]),
            (v_bpr, [(col_bp, 16, 1.0)]),
            (v_wsdp, [(col_wsd, 16, ISI)]),
            (v_sudp, [(col_sdp, 64, 1.0)]),
            (v_wsug, [(col_wsu, 64, ISD), (col_wsg, 64, ISD)]),
        ]
        tpi = 0
        for src, dests in colplan:
            flat = []
            for dst, n, scl in dests:
                for j in range(n):
                    flat.append((dst, j, scl))
            total = len(flat)
            for ch in range((total + 3) // 4):
                ncol = min(4, total - ch * 4)
                sch.wait("sync", "dv", C["dv"])
                sch.dma("sync", vecs[0:1, 0:ncol * 128],
                        src[0:1, ch * 512: ch * 512 + ncol * 128], "qs")
                qs_here = C["qs"]
                sch.wait("tensor", "qs", qs_here)
                for j in range(ncol):
                    dst, colj, scl = flat[ch * 4 + j]
                    bank = pf[tpi % 2]
                    sch.wait("tensor", "dv", C["dv"])
                    sch.emit("tensor", "transpose", bank[:, 0:1],
                             vecs[0:1, j * 128:(j + 1) * 128], c_idf[0:1, 0:1],
                             inc="pe")
                    sch.wait("vector", "pe", C["pe"])
                    sch.emit("vector", "tensor_scalar", dst[:, colj:colj + 1],
                             bank[:, 0:1], scl, None, ALU.mult, inc="dv")
                    tpi += 1
        for ub in range(NU):
            sch.emit("vector", "tensor_mul", col_wsu[:, ub:ub + 1],
                     col_wsu[:, ub:ub + 1], col_sdp[:, ub:ub + 1], inc="dv")

        prep_state = {
            "xb_pe": {0: 0, 1: 0},
            "reg_qs": {0: 0, 1: 0},
            "pt_dv": {0: 0, 1: 0},
            "idx": 0,
        }

        def prep_slab(slab_ap, fcols, dst_ap, stages):
            idx = prep_state["idx"]
            prep_state["idx"] += 1
            if fcols <= 2048:
                reg = idx % 2
                s2 = strips[:, reg * 4096: reg * 4096 + fcols]
                s3 = strips[:, reg * 4096 + 2048: reg * 4096 + 2048 + fcols]
            else:
                reg = 0
                s2 = strips[:, 4096:4096 + fcols]
                s3 = None
            nchunk = (fcols + 2047) // 2048
            for q in range(nchunk):
                w = min(2048, fcols - q * 2048)
                xslot = (idx + q) % 2
                xb = xbf[:, xslot * 2048: xslot * 2048 + w]
                sch.wait("gpsimd", "pe", prep_state["xb_pe"][xslot])
                sch.dma("gpsimd", xb, slab_ap[:, q * 2048: q * 2048 + w],
                        f"qx{xslot}")
                sch.wait("tensor", f"qx{xslot}", C[f"qx{xslot}"])
                for half in range(w // 1024):
                    bank = pt[half % 2]
                    sch.wait("tensor", "dv", prep_state["pt_dv"][half % 2])
                    for t in range(8):
                        ti = half * 8 + t
                        sch.emit("tensor", "transpose", bank[:, t * 128:(t + 1) * 128],
                                 xb[:, ti * 128:(ti + 1) * 128], c_idb[:, :],
                                 inc="pe" if t == 7 else None)
                    sch.wait("vector", "pe", C["pe"])
                    if q == 0 and half == 0:
                        if fcols <= 2048:
                            sch.wait("vector", f"qpr{reg}",
                                     prep_state["reg_qs"][reg])
                        else:
                            sch.wait("vector", "qpr0", C["qpr0"])
                            sch.wait("vector", "qpr1", C["qpr1"])
                    sch.emit("vector", "tensor_copy",
                             s2[:, q * 2048 + half * 1024: q * 2048 + half * 1024 + 1024],
                             bank[:, 0:1024], inc="dv")
                    prep_state["pt_dv"][half % 2] = C["dv"]
                prep_state["xb_pe"][xslot] = C["pe"]
            if stages:
                if fcols <= 2048:
                    cur, oth = s2, s3
                    for st in range(stages):
                        half = 2 ** st
                        va = cur.rearrange("p (q t r) -> p q t r", t=2,
                                           r=half * 128)
                        vo = oth.rearrange("p (q t r) -> p q t r", t=2,
                                           r=half * 128)
                        sch.emit("vector", "tensor_add", vo[:, :, 0, :],
                                 va[:, :, 0, :], va[:, :, 1, :], inc="dv")
                        sch.emit("vector", "tensor_sub", vo[:, :, 1, :],
                                 va[:, :, 0, :], va[:, :, 1, :], inc="dv")
                        cur, oth = oth, cur
                    src_final = cur
                else:
                    for st in range(stages):
                        half = 2 ** st
                        va = s2.rearrange("p (q t r) -> p q t r", t=2,
                                          r=half * 128)
                        nq = fcols // (2 * half * 128)
                        span = half * 128
                        for qq in range(nq):
                            for cc0 in range(0, span, 1024):
                                w2 = min(1024, span - cc0)
                                a = va[:, qq, 0, cc0:cc0 + w2]
                                bb = va[:, qq, 1, cc0:cc0 + w2]
                                t2 = kstg[:, 0:w2]
                                sch.emit("vector", "tensor_copy", t2, a)
                                sch.emit("vector", "tensor_add", a, a, bb)
                                sch.emit("vector", "tensor_sub", bb, t2, bb,
                                         inc="dv")
                    src_final = s2
            else:
                src_final = s2
            sch.wait("sync", "dv", C["dv"])
            sch.dma("sync", dst_ap, src_final, f"qpr{reg}")
            prep_state["reg_qs"][reg] = C[f"qpr{reg}"]

        for cb in range(6):
            prep_slab(wq_sl[cb * 128:(cb + 1) * 128, :], D,
                      sh_attn[cb * 128:(cb + 1) * 128, :], 0)
        for cb in range(2):
            prep_slab(wp_sl[cb * 128:(cb + 1) * 128, :], D,
                      sh_attn[768 + cb * 128: 768 + (cb + 1) * 128, :], 0)
        sch.wait("gpsimd", "qpr0", C["qpr0"])
        sch.wait("gpsimd", "qpr1", C["qpr1"])
        sch.emit("gpsimd", "collective_compute", "AllGather", ALU.bypass,
                 replica_groups=RG, ins=[sh_attn[:, :]], outs=[g_attn[:, :]],
                 inc="cc")
        cc_attn = C["cc"]
        stop = _phase_end("p1")

        def emit_ln(srcs, dsts, su_fold):
            sch.wait("tensor", "dv", C["dv"])
            ln_entry_pe = C["pe"]
            sumsq_pe = {}
            for fb in range(NF):
                sch.emit("tensor", "matmul", pf[4][0:1, :], c_ones[:, 0:1],
                         srcs[fb], start=(fb == 0), stop=(fb == NF - 1))
            for fb in range(NF):
                sch.wait("vector", "pe", sumsq_pe.get(fb - 4, ln_entry_pe))
                sch.emit("vector", "tensor_mul",
                         sq[:, (fb % 4) * 512:(fb % 4) * 512 + 512],
                         srcs[fb], srcs[fb], inc="dv")
                sch.wait("tensor", "dv", C["dv"])
                sch.emit("tensor", "matmul", pf[5][0:1, :], c_ones[:, 0:1],
                         sq[:, (fb % 4) * 512:(fb % 4) * 512 + 512],
                         start=(fb == 0), stop=(fb == NF - 1), inc="pe")
                sumsq_pe[fb] = C["pe"]
            pe_sums = C["pe"]
            sch.wait("vector", "pe", pe_sums)
            sch.emit("vector", "tensor_scalar", row_a[0:1, :], pf[4][0:1, :],
                     1.0 / D, None, ALU.mult)
            sch.emit("vector", "tensor_copy", row_bf[0:1, 0:512], row_a[0:1, :])
            sch.emit("vector", "tensor_mul", row_b[0:1, :], row_a[0:1, :],
                     row_a[0:1, :])
            sch.emit("vector", "scalar_tensor_tensor", row_b[0:1, :],
                     pf[5][0:1, :], 1.0 / D, row_b[0:1, :], ALU.mult,
                     ALU.subtract, inc="dv")
            sch.wait("scalar", "dv", C["dv"])
            sch.emit("scalar", "activation", row_b[0:1, :], row_b[0:1, :],
                     AF.Sqrt, bias=eps1[0:1, 0:1], inc="ac")
            sch.wait("vector", "ac", C["ac"])
            sch.emit("vector", "reciprocal", row_a[0:1, :], row_b[0:1, :])
            sch.emit("vector", "tensor_copy", row_bf[0:1, 512:1024],
                     row_a[0:1, :], inc="dv")
            sch.wait("tensor", "dv", C["dv"])
            sch.emit("tensor", "matmul", pf[4][:, :], c_ones[0:1, :],
                     row_bf[0:1, 0:512], start=True, stop=True, inc="pe")
            sch.wait("vector", "pe", C["pe"])
            sch.emit("vector", "tensor_copy", mb[:, :], pf[4][:, :], inc="dv")
            sch.wait("tensor", "dv", C["dv"])
            sch.emit("tensor", "matmul", pf[4][:, :], c_ones[0:1, :],
                     row_bf[0:1, 512:1024], start=True, stop=True, inc="pe")
            sch.wait("vector", "pe", C["pe"])
            sch.emit("vector", "tensor_copy", rb[:, :], pf[4][:, :], inc="dv")
            for fb in range(NF):
                sch.emit("vector", "tensor_sub", dsts[fb], srcs[fb], mb[:, :])
                if su_fold is not None:
                    sch.emit("vector", "scalar_tensor_tensor", dsts[fb],
                             dsts[fb], su_fold(fb), rb[:, :], ALU.mult,
                             ALU.mult, inc="dv")
                else:
                    sch.emit("vector", "tensor_mul", dsts[fb], dsts[fb],
                             rb[:, :], inc="dv")

        for tb in range(B):
            sch.wait("vector", "pe", C["pe"])
            for tt in range(4):
                par = tt % 2
                xb = xbf[:, par * 2048:(par + 1) * 2048]
                sch.wait("gpsimd", "dv", C["dv"])
                sch.dma("gpsimd", xb, x_loc[tb, tt * 128:(tt + 1) * 128, :],
                        f"qx{par}")
                sch.wait("tensor", f"qx{par}", C[f"qx{par}"])
                b2v = big2.rearrange("p (f q) -> p f q", q=512)
                for half in range(2):
                    bank = pt[half]
                    sch.wait("tensor", "dv", C["dv"])
                    for t in range(8):
                        fb = half * 8 + t
                        sch.emit("tensor", "transpose",
                                 bank[:, t * 128:(t + 1) * 128],
                                 xb[:, fb * 128:(fb + 1) * 128], c_idb[:, :],
                                 inc="pe" if t == 7 else None)
                    sch.wait("vector", "pe", C["pe"])
                    dstv = b2v[:, half * 8:(half + 1) * 8,
                               tt * 128:(tt + 1) * 128]
                    srcv = bank.rearrange("p (t c) -> p t c", c=128)
                    sch.emit("vector", "tensor_copy", dstv, srcv, inc="dv")
            emit_ln([big2c(fb) for fb in range(NF)],
                    [xln(fb, tb) for fb in range(NF)], None)
        stop = stop or _phase_end("p2")

        group_pe = {}
        evac_dv = {}
        kq = {0: 0, 1: 0}
        vq = {0: 0, 1: 0}
        vt_pe = {0: 0, 1: 0}
        kwr = 0
        vwr = 0
        cc_kv = {}
        sch.wait("tensor", "dv", C["dv"])
        for hpass in range(2):
            tbs = (2 * hpass, 2 * hpass + 1)
            for cb in range(48):
                h, pp = cb // 3, cb % 3
                sp = cb % 2
                r0 = (cb // 6) * 1024 + (cb % 6) * 128
                key = (hpass, cb)
                sch.wait("sync", "pe", group_pe.get((hpass, cb - 2),
                                                    group_pe.get((hpass - 1, 47), 0)))
                if (hpass, cb) == (0, 0):
                    sch.wait("sync", "cc", cc_attn)
                sch.dma("sync", strip_slot(sp), g_attn[r0:r0 + 128, :],
                        f"qw{sp}")
                sch.wait("tensor", f"qw{sp}", C[f"qw{sp}"])
                for tb in tbs:
                    sch.wait("tensor", "dv",
                             evac_dv.get((hpass, cb - 2, tb), 0))
                for fb in range(NF):
                    for tb in tbs:
                        bank = pf[(cb % 2) * 2 + (tb % 2)]
                        sch.emit("tensor", "matmul", bank[:, :],
                                 strip_slot(sp)[:, fb * 128:(fb + 1) * 128],
                                 xln(fb, tb), start=(fb == 0),
                                 stop=(fb == NF - 1),
                                 inc="pe" if fb == NF - 1 else None)
                group_pe[key] = C["pe"]
                for tb in tbs:
                    bank = pf[(cb % 2) * 2 + (tb % 2)]
                    g = tb * 16 + h
                    sch.wait("vector", "pe", group_pe[key])
                    if pp == 0:
                        sch.emit("vector", "tensor_copy", U(g), bank[:, :],
                                 inc="dv")
                    elif pp == 1:
                        par = kwr % 2
                        ks = kstg[:, par * 512:(par + 1) * 512]
                        sch.wait("vector", f"qks{par}", kq[par])
                        sch.emit("vector", "tensor_copy", ks, bank[:, :],
                                 inc="dv")
                        sch.wait("gpsimd", "dv", C["dv"])
                        sch.dma("gpsimd", kv_sh[g * 128:(g + 1) * 128, 0:512],
                                ks, f"qks{par}")
                        kq[par] = C[f"qks{par}"]
                        kwr += 1
                    else:
                        par = vwr % 2
                        vt = vtmp[:, par * 512:(par + 1) * 512]
                        vs = vstg[:, (vwr % 2) * 528:(vwr % 2) * 528 + 528]
                        sch.wait("vector", "pe", vt_pe[par])
                        sch.emit("vector", "tensor_copy", vt, bank[:, :],
                                 inc="dv")
                        dv_vt = C["dv"]
                        tbank = pt[vwr % 2]
                        sch.wait("tensor", "dv", dv_vt)
                        for sh in range(4):
                            sch.emit("tensor", "transpose",
                                     tbank[:, sh * 128:(sh + 1) * 128],
                                     vt[:, sh * 128:(sh + 1) * 128],
                                     c_idb[:, :],
                                     inc="pe" if sh == 3 else None)
                        sch.wait("vector", "pe", C["pe"])
                        sch.wait("vector", f"qvs{vwr % 2}", vq[vwr % 2])
                        vsv = vs.rearrange("p (s c) -> p s c", c=132)
                        sch.emit("vector", "tensor_copy", vsv[:, :, 0:128],
                                 tbank.rearrange("p (t c) -> p t c",
                                                 c=128)[:, 0:4, :],
                                 inc="dv")
                        vt_pe[par] = C["pe"]
                        sch.wait("gpsimd", "dv", C["dv"])
                        sch.dma("gpsimd",
                                kv_sh[g * 128:(g + 1) * 128, 512:1040],
                                vs, f"qvs{vwr % 2}")
                        vq[vwr % 2] = C[f"qvs{vwr % 2}"]
                        vwr += 1
                    evac_dv[(hpass, cb, tb)] = C["dv"]
            if not stop:
                sch.wait("gpsimd", "qks0", C["qks0"])
                sch.wait("gpsimd", "qks1", C["qks1"])
                sch.wait("gpsimd", "qvs0", C["qvs0"])
                sch.wait("gpsimd", "qvs1", C["qvs1"])
                gdst = g_kv_lo if hpass == 0 else g_kv_hi
                gsrc = kv_sh[hpass * 4096:(hpass + 1) * 4096, :]
                sch.emit("gpsimd", "collective_compute", "AllGather",
                         ALU.bypass, replica_groups=RG, ins=[gsrc],
                         outs=[gdst[:, :]], inc="cc")
                cc_kv[hpass] = C["cc"]
        stop = stop or _phase_end("p3")

        for ub in range(8 if not stop else 0):
            prep_slab(up_sl[ub * 128:(ub + 1) * 128, :], D,
                      sh_ffn[ub * 128:(ub + 1) * 128, :], 4)
        for ub in range(8 if not stop else 0):
            prep_slab(gt_sl[ub * 128:(ub + 1) * 128, :], D,
                      sh_ffn[1024 + ub * 128: 1024 + (ub + 1) * 128, :], 4)
        for ob in range(2 if not stop else 0):
            dn_dst = sh_ffn[2048 + ob * 512: 2048 + (ob + 1) * 512,
                            :].rearrange("(fi q) c -> fi (q c)", q=4)
            prep_slab(dn_sl[ob * 128:(ob + 1) * 128, :], DI, dn_dst, 6)
        if not stop:
            sch.wait("gpsimd", "qpr0", C["qpr0"])
            sch.wait("gpsimd", "qpr1", C["qpr1"])
            sch.emit("gpsimd", "collective_compute", "AllGather", ALU.bypass,
                     replica_groups=RG, ins=[sh_ffn[:, :]], outs=[g_ffn[:, :]],
                     inc="cc")
        cc_ffn = C["cc"]
        stop = stop or _phase_end("p1b")

        LAG = 3
        kload = {}
        vload = {}
        scores_pe = {}
        expcnt = {}
        den_pe = {}
        ctx_pe = {}
        recip_dv = {}
        mul_dv = {}
        bc_copy_dv = {}
        sch.wait("tensor", "dv", C["dv"])
        sch.wait("tensor", "ac", C["ac"])
        p3_pe_done = C["pe"]

        def emit_kv_load(g):
            p = g % 2
            if g >= 2:
                sch.wait("sync", "pe", scores_pe[(g - 2, NKT - 1)])
                sch.wait("scalar", "pe", ctx_pe[g - 2])
            else:
                sch.wait("sync", "pe", p3_pe_done)
                sch.wait("scalar", "pe", p3_pe_done)
            cw = cc_kv[0] if g < 32 else cc_kv[1]
            sch.wait("sync", "cc", cw)
            sch.wait("scalar", "cc", cw)
            sch.dma("sync", KB(p), k_view(g), f"qk{p}")
            kload[g] = C[f"qk{p}"]
            sch.dma("scalar", VB(p), v_view(g), f"qv{p}")
            vload[g] = C[f"qv{p}"]

        def emit_ctx_den(g, kt):
            p = g % 2
            sch.wait("tensor", "ac", expcnt[(g, kt)])
            if kt == 0:
                sch.wait("tensor", f"qv{p}", vload[g])
                sch.wait("tensor", "dv", mul_dv.get(g - 2, 0))
            sch.emit("tensor", "matmul", pf[2 + p][:, :],
                     VB(p)[:, kt * 132: kt * 132 + 128], PT4(kt % 4),
                     start=(kt == 0), stop=(kt == NKT - 1),
                     inc="pe" if kt == NKT - 1 else None)
            if kt == NKT - 1:
                ctx_pe[g] = C["pe"]
            if kt == 0:
                sch.wait("tensor", "dv", recip_dv.get(g - 1, 0))
            sch.emit("tensor", "matmul", pf[4][0:1, :], c_ones[:, 0:1],
                     PT4(kt % 4), start=(kt == 0), stop=(kt == NKT - 1),
                     inc="pe" if kt == NKT - 1 else None)
            den_pe[(g, kt)] = C["pe"]

        if not stop:
            emit_kv_load(0)
            emit_kv_load(1)
        for g in range(NG if not stop else 0):
            p = g % 2
            sch.wait("tensor", f"qk{p}", kload[g])
            for kt in range(NKT):
                prev = (g, kt - 2) if kt >= 2 else (g - 1, NKT - 2 + kt)
                sch.wait("tensor", "ac", expcnt.get(prev, 0))
                sch.emit("tensor", "matmul", pf[kt % 2][:, :],
                         KB(p)[:, kt * 128:(kt + 1) * 128], U(g),
                         start=True, stop=True, inc="pe")
                scores_pe[(g, kt)] = C["pe"]
                sch.wait("scalar", "pe", scores_pe[(g, kt)])
                sch.emit("scalar", "activation", PT4(kt % 4), pf[kt % 2][:, :],
                         AF.Exp, scale=SCALE_QK, inc="ac")
                expcnt[(g, kt)] = C["ac"]
                if kt >= LAG:
                    emit_ctx_den(g, kt - LAG)
            for kt in range(NKT - LAG, NKT):
                emit_ctx_den(g, kt)
            sch.wait("vector", "pe", den_pe[(g, NKT - 1)])
            sch.emit("vector", "reciprocal", row_b[0:1, :], pf[4][0:1, :])
            sch.emit("vector", "tensor_copy", row_bf[0:1, 0:512],
                     row_b[0:1, :], inc="dv")
            recip_dv[g] = C["dv"]
            sch.wait("tensor", "dv", recip_dv[g])
            sch.wait("tensor", "dv", bc_copy_dv.get(g - 1, 0))
            sch.emit("tensor", "matmul", pf[5][:, :], c_ones[0:1, :],
                     row_bf[0:1, 0:512], start=True, stop=True, inc="pe")
            sch.wait("vector", "pe", C["pe"])
            sch.emit("vector", "tensor_copy", bca[:, :], pf[5][:, :],
                     inc="dv")
            bc_copy_dv[g] = C["dv"]
            sch.emit("vector", "tensor_mul", U(g), pf[2 + p][:, :], bca[:, :],
                     inc="dv")
            mul_dv[g] = C["dv"]
            if g + 2 < NG:
                emit_kv_load(g + 2)

        stop = stop or _phase_end("p4")
        op_group_pe = {}
        op_evac_dv = {}
        sch.wait("tensor", "dv", C["dv"])
        sch.wait("tensor", "ac", C["ac"])
        for bp in range(B if not stop else 0):
            sch.wait("sync", "pe", C["pe"])
            sch.wait("vector", "pe", C["pe"])
            for tt in range(4):
                par = tt % 2
                xb = xbf[:, par * 2048:(par + 1) * 2048]
                sch.wait("gpsimd", "dv", C["dv"])
                sch.dma("gpsimd", xb, x_loc[bp, tt * 128:(tt + 1) * 128, :],
                        f"qx{par}")
                sch.wait("tensor", f"qx{par}", C[f"qx{par}"])
                b2v = big2.rearrange("p (f q) -> p f q", q=512)
                for half in range(2):
                    bank = pt[half]
                    sch.wait("tensor", "dv", C["dv"])
                    for t in range(8):
                        fb = half * 8 + t
                        sch.emit("tensor", "transpose",
                                 bank[:, t * 128:(t + 1) * 128],
                                 xb[:, fb * 128:(fb + 1) * 128], c_idb[:, :],
                                 inc="pe" if t == 7 else None)
                    sch.wait("vector", "pe", C["pe"])
                    dstv = b2v[:, half * 8:(half + 1) * 8,
                               tt * 128:(tt + 1) * 128]
                    srcv = bank.rearrange("p (t c) -> p t c", c=128)
                    sch.emit("vector", "tensor_copy", dstv, srcv, inc="dv")
            for cb in range(NF):
                sp = cb % 2
                r0 = (cb // 2) * 1024 + 768 + (cb % 2) * 128
                sch.wait("sync", "pe", op_group_pe.get((bp, cb - 2), 0))
                sch.dma("sync", strip_slot(sp), g_attn[r0:r0 + 128, :],
                        f"qw{sp}")
                sch.wait("tensor", f"qw{sp}", C[f"qw{sp}"])
                sch.wait("tensor", "dv", op_evac_dv.get((bp, cb - 4), 0))
                for j in range(NF):
                    sch.emit("tensor", "matmul", pf[cb % 4][:, :],
                             strip_slot(sp)[:, j * 128:(j + 1) * 128],
                             U(4 * j + bp), start=(j == 0),
                             stop=(j == NF - 1),
                             inc="pe" if j == NF - 1 else None)
                op_group_pe[(bp, cb)] = C["pe"]
                sch.wait("vector", "pe", C["pe"])
                sch.emit("vector", "scalar_tensor_tensor", r1(cb, bp),
                         pf[cb % 4][:, :], col_bp[:, cb:cb + 1], big2c(cb),
                         ALU.add, ALU.add, inc="dv")
                op_evac_dv[(bp, cb)] = C["dv"]

        z_ac = {}
        ug_pe = {}
        silu_ac = {}
        y_dv = {}
        z2_ac = {}
        dn_pe = {}
        dn_chunk_pe = {}
        out_dv = {}
        out_qp = {}
        stop = stop or _phase_end("p5")
        if not stop:
            sch.wait("sync", "cc", cc_ffn)
            sch.wait("scalar", "cc", cc_ffn)
        for tb in range(B if not stop else 0):
            sch.wait("tensor", "dv", C["dv"])
            sch.wait("tensor", "ac", C["ac"])
            sch.wait("sync", "pe", C["pe"])
            sch.wait("scalar", "pe", C["pe"])
            sch.wait("vector", "pe", C["pe"])
            emit_ln([r1(fb, tb) for fb in range(NF)],
                    [big2c(fb) for fb in range(NF)],
                    lambda fb: col_su[:, fb:fb + 1])
            for cix in range(NF):
                bank = pf[cix % 4]
                sch.wait("tensor", "dv", z_ac.get((tb, cix - 4), C["dv"] if cix == 0 else 0))
                sch.emit("tensor", "matmul", bank[:, :], c_h128[:, :],
                         big2c(cix), start=True, stop=True, inc="pe")
                sch.wait("vector", "pe", C["pe"])
                sch.emit("vector", "tensor_copy", big2c(cix), bank[:, :],
                         inc="dv")
                z_ac[(tb, cix)] = C["dv"]
            z_done = C["dv"]
            for ub in range(NU):
                r = ub // 8
                ru = r * 3072 + (ub % 8) * 128
                rg = r * 3072 + 1024 + (ub % 8) * 128
                sch.wait("sync", "pe", ug_pe.get((tb, ub - 3), 0))
                sch.dma("sync", strip_slot(ub % 3), g_ffn[ru:ru + 128, :],
                        f"qu{ub % 3}")
                up_qs = C[f"qu{ub % 3}"]
                sch.wait("scalar", "pe", ug_pe.get((tb, ub - 3), 0))
                sch.dma("scalar", strip_slot(3 + ub % 3),
                        g_ffn[rg:rg + 128, :], f"qg{ub % 3}")
                gt_qa = C[f"qg{ub % 3}"]
                if ub == 0:
                    sch.wait("tensor", "dv", z_done)
                sch.wait("tensor", f"qu{ub % 3}", up_qs)
                sch.wait("tensor", f"qg{ub % 3}", gt_qa)
                sch.wait("tensor", "dv", y_dv.get((tb, ub - 2), 0))
                sch.wait("tensor", "ac", silu_ac.get((tb, ub - 2), 0))
                bu = pf[ub % 2]
                bg = pf[2 + ub % 2]
                for fb in range(NF):
                    sch.emit("tensor", "matmul", bu[:, :],
                             strip_slot(ub % 3)[:, fb * 128:(fb + 1) * 128],
                             big2c(fb), start=(fb == 0), stop=(fb == NF - 1),
                             inc="pe" if fb == NF - 1 else None)
                for fb in range(NF):
                    sch.emit("tensor", "matmul", bg[:, :],
                             strip_slot(3 + ub % 3)[:, fb * 128:(fb + 1) * 128],
                             big2c(fb), start=(fb == 0), stop=(fb == NF - 1),
                             inc="pe" if fb == NF - 1 else None)
                ug_pe[(tb, ub)] = C["pe"]
                sch.wait("scalar", "pe", C["pe"])
                sch.wait("scalar", "dv", y_dv.get((tb, ub - 2), 0))
                sg = sgb[:, (ub % 2) * 512:(ub % 2) * 512 + 512]
                sch.emit("scalar", "activation", sg, bg[:, :], AF.Silu,
                         scale=col_wsg[:, ub:ub + 1], inc="ac")
                silu_ac[(tb, ub)] = C["ac"]
                sch.wait("vector", "ac", C["ac"])
                sch.emit("vector", "scalar_tensor_tensor",
                         big1[:, ub * 512:(ub + 1) * 512], bu[:, :],
                         col_wsu[:, ub:ub + 1], sg, ALU.mult, ALU.mult,
                         inc="dv")
                y_dv[(tb, ub)] = C["dv"]
            for cix in range(NU):
                bank = pf[2 + cix % 4]
                if cix == 0:
                    sch.wait("tensor", "ac", silu_ac[(tb, NU - 1)])
                sch.wait("tensor", "dv", y_dv[(tb, cix)])
                sch.wait("tensor", "dv", z2_ac.get((tb, cix - 4), 0))
                sch.emit("tensor", "matmul", bank[:, :], c_h128[:, :],
                         big1[:, cix * 512:(cix + 1) * 512], start=True,
                         stop=True, inc="pe")
                sch.wait("vector", "pe", C["pe"])
                sch.emit("vector", "tensor_copy",
                         big1[:, cix * 512:(cix + 1) * 512], bank[:, :],
                         inc="dv")
                z2_ac[(tb, cix)] = C["dv"]
            z2_done = C["dv"]
            for ob in range(NF):
                r = ob // 2
                base = r * 3072 + 2048 + (ob % 2) * 512
                dview = g_ffn[base:base + 512, :].rearrange(
                    "(fi q) c -> q fi c", q=4)
                chunk_qs = []
                for q in range(4):
                    if ob > 0:
                        sch.wait("sync", "pe", dn_chunk_pe[(tb, ob - 1, q)])
                    else:
                        sch.wait("sync", "pe", ug_pe[(tb, NU - 1)])
                    sch.dma("sync", strip_slot(q), dview[q], f"qd{q}")
                    chunk_qs.append(C[f"qd{q}"])
                bank = pf[ob % 2]
                if ob == 0:
                    sch.wait("tensor", "dv", y_dv[(tb, NU - 1)])
                    sch.wait("tensor", "dv", z2_done)
                sch.wait("tensor", "dv", out_dv.get((tb, ob - 2), 0))
                for fb in range(NU):
                    q = fb // 16
                    if fb % 16 == 0:
                        sch.wait("tensor", f"qd{q}", chunk_qs[q])
                    sch.emit("tensor", "matmul", bank[:, :],
                             strip_slot(q)[:, (fb % 16) * 128:
                                           (fb % 16) * 128 + 128],
                             big1[:, fb * 512:(fb + 1) * 512],
                             start=(fb == 0), stop=(fb == NU - 1),
                             inc="pe" if fb % 16 == 15 else None)
                    if fb % 16 == 15:
                        dn_chunk_pe[(tb, ob, q)] = C["pe"]
                dn_pe[(tb, ob)] = C["pe"]
                sch.wait("vector", "pe", C["pe"])
                sch.emit("vector", "scalar_tensor_tensor", big2c(ob),
                         bank[:, :], col_wsd[:, ob:ob + 1], r1(ob, tb),
                         ALU.mult, ALU.add, inc="dv")
                out_dv[(tb, ob)] = C["dv"]
            for tt in range(4):
                stg = stg_out[:, 0:2048]
                for half in range(2):
                    bank = pt[half]
                    sch.wait("tensor", "dv", C["dv"])
                    for t in range(8):
                        ob = half * 8 + t
                        sch.emit("tensor", "transpose",
                                 bank[:, t * 128:(t + 1) * 128],
                                 big2c(ob, tt), c_idb[:, :],
                                 inc="pe" if t == 7 else None)
                    sch.wait("vector", "pe", C["pe"])
                    if half == 0:
                        sch.wait("vector", "qp",
                                 out_qp.get((tb * 4 + tt) - 1, 0))
                    sch.emit("vector", "tensor_copy",
                             stg[:, half * 1024:(half + 1) * 1024],
                             bank[:, 0:1024], inc="dv")
                sch.wait("gpsimd", "dv", C["dv"])
                sch.dma("gpsimd", out_loc[tb, tt * 128:(tt + 1) * 128, :],
                        stg, "qp")
                out_qp[tb * 4 + tt] = C["qp"]

        sch.wait("sync", "qp", C["qp"])
        sch.wait("sync", "qs", C["qs"])
        sch.wait("sync", "qa", C["qa"])
        sch.wait("sync", "qk0", C["qk0"])
        sch.wait("sync", "qk1", C["qk1"])
        sch.wait("sync", "qv0", C["qv0"])
        sch.wait("sync", "qv1", C["qv1"])
        sch.wait("sync", "qw0", C["qw0"])
        sch.wait("sync", "qw1", C["qw1"])
        sch.wait("sync", "qu0", C["qu0"])
        sch.wait("sync", "qu1", C["qu1"])
        sch.wait("sync", "qg0", C["qg0"])
        sch.wait("sync", "qg1", C["qg1"])
        sch.wait("sync", "qd0", C["qd0"])
        sch.wait("sync", "qd1", C["qd1"])
        sch.wait("sync", "qd2", C["qd2"])
        sch.wait("sync", "qd3", C["qd3"])
        sch.wait("sync", "qx0", C["qx0"])
        sch.wait("sync", "qx1", C["qx1"])
        sch.wait("sync", "qpr0", C["qpr0"])
        sch.wait("sync", "qpr1", C["qpr1"])
        sch.wait("sync", "qks0", C["qks0"])
        sch.wait("sync", "qks1", C["qks1"])
        sch.wait("sync", "qvs0", C["qvs0"])
        sch.wait("sync", "qvs1", C["qvs1"])
        sch.wait("gpsimd", "qp", C["qp"])
        if dump:
            sch.wait("sync", "pe", C["pe"])
            sch.wait("sync", "dv", C["dv"])
            sch.wait("sync", "ac", C["ac"])
            sch.wait("sync", "cc", C["cc"])
            sch.dma("sync", d_big1[:, :], big1[:], "qs")
            sch.dma("sync", d_big2[:, :], big2[:], "qs")
            sch.dma("sync", d_gattn[:, :], g_attn[:, :], "qs")
            sch.dma("sync", d_kvsh[:, :], kv_sh[:, :], "qs")
            sch.dma("sync", d_gffn[:, :], g_ffn[:, :], "qs")
            sch.wait("sync", "qs", C["qs"])

        sch.check()
        build.last_sched = sch

        block = ctx.enter_context(nc.Block())

        @block.sync
        def _(eng):
            sch.replay("sync", eng, sems)

        @block.vector
        def _(eng):
            sch.replay("vector", eng, sems)

        @block.scalar
        def _(eng):
            sch.replay("scalar", eng, sems)

        @block.tensor
        def _(eng):
            sch.replay("tensor", eng, sems)

        @block.gpsimd
        def _(eng):
            sch.replay("gpsimd", eng, sems)

    return nc


def kernel(**inputs):
    import ml_dtypes
    x = np.ascontiguousarray(np.asarray(inputs["L0_attn_query"], np.float32))
    Wqkv = np.ascontiguousarray(np.asarray(inputs["L0_attn_Wqkv"], np.float32))
    Wproj = np.ascontiguousarray(np.asarray(inputs["L0_attn_Wproj"], np.float32))
    bproj = np.ascontiguousarray(
        np.asarray(inputs["L0_attn_bproj"], np.float32).reshape(1, D))
    su_ug = np.ascontiguousarray(
        np.asarray(inputs["L0_ffn_SU_ug"], np.float32).reshape(1, D))
    su_dp = np.ascontiguousarray(
        np.asarray(inputs["L0_ffn_SU_dp"], np.float32).reshape(1, DI))
    ws_ug = np.ascontiguousarray(
        np.asarray(inputs["L0_ffn_Wscale_ug"], np.float32).reshape(1, 2 * DI))
    ws_dp = np.ascontiguousarray(
        np.asarray(inputs["L0_ffn_Wscale_dp"], np.float32).reshape(1, D))
    W_up = np.ascontiguousarray(np.asarray(inputs["L0_ffn_W_up"], np.float32))
    W_gate = np.ascontiguousarray(np.asarray(inputs["L0_ffn_W_gate"], np.float32))
    W_down = np.ascontiguousarray(np.asarray(inputs["L0_ffn_W_down"], np.float32))

    h128 = _hadamard(128).astype(ml_dtypes.bfloat16)
    idb = np.eye(128, dtype=ml_dtypes.bfloat16)
    idf = np.eye(128, dtype=np.float32)
    onesb = np.ones((128, 128), dtype=ml_dtypes.bfloat16)

    in_maps = []
    for c in range(NCORES):
        in_maps.append({
            "x_loc": np.ascontiguousarray(x[:, c * SC:(c + 1) * SC, :]),
            "wq_sl": np.ascontiguousarray(Wqkv[768 * c: 768 * (c + 1), :]),
            "wp_sl": np.ascontiguousarray(Wproj[256 * c: 256 * (c + 1), :]),
            "up_sl": np.ascontiguousarray(W_up[1024 * c: 1024 * (c + 1), :]),
            "gt_sl": np.ascontiguousarray(W_gate[1024 * c: 1024 * (c + 1), :]),
            "dn_sl": np.ascontiguousarray(W_down[256 * c: 256 * (c + 1), :]),
            "v_suug": su_ug, "v_sudp": su_dp, "v_wsug": ws_ug,
            "v_wsdp": ws_dp, "v_bpr": bproj,
            "c_h128": h128, "c_idb": idb, "c_idf": idf, "c_onesb": onesb,
        })

    nc = build()
    res = run_bass_kernel_spmd(nc, in_maps, core_ids=list(range(NCORES)))
    out = np.empty((B, S, D), np.float32)
    for c in range(NCORES):
        out[:, c * SC:(c + 1) * SC, :] = res.results[c]["out_loc"]
    return out


if __name__ == "__main__":
    import jax
    jax.config.update("jax_default_device", jax.devices("cpu")[0])
    import reference as refmod
    inp = refmod.setup_inputs(0)
    out = kernel(**{k: np.asarray(v) for k, v in inp.items()})
    ref = np.asarray(refmod.reference(**inp))
    d = out - ref
    rvr = float((d ** 2).mean() / (ref.astype(np.float64) ** 2).mean())
    print("rvr", rvr, "rel", float(np.linalg.norm(d) / np.linalg.norm(ref)))


# version 13
# speedup vs baseline: 1.0153x; 1.0153x over previous
import numpy as np
from contextlib import ExitStack

import concourse.bass as bass
import concourse.mybir as mybir
from concourse.bass_utils import run_bass_kernel_spmd

F32 = mybir.dt.float32
BF16 = mybir.dt.bfloat16
AF = mybir.ActivationFunctionType
ALU = mybir.AluOpType

NCORES = 8
RG = [[0, 1, 2, 3, 4, 5, 6, 7]]
B, S, D = 4, 4096, 2048
SC = S // NCORES
NH, HN = 16, 128
DI = 8192
NF = D // 128
NU = DI // 128
NG = B * NH
NKT = S // 128
SCALE_QK = float(1.0 / np.sqrt(np.float64(HN)))
ISD = float(1.0 / np.sqrt(np.float64(D)))
ISI = float(1.0 / np.sqrt(np.float64(DI)))
EPS = 1e-5

ENGINES = ("sync", "vector", "scalar", "tensor", "gpsimd")


def _hadamard(n):
    h = np.ones((1, 1), np.float32)
    while h.shape[0] < n:
        h = np.block([[h, h], [h, -h]])
    return h


class Sched:

    def __init__(self):
        self.ops = {e: [] for e in ENGINES}
        self.cnt = {k: 0 for k in (
            "qs", "qa", "qp", "cc", "pe", "dv", "ac",
            "qk0", "qk1", "qv0", "qv1", "qw0", "qw1", "qu0", "qu1", "qu2",
            "qg0", "qg1", "qg2", "qd0", "qd1", "qd2", "qd3", "qx0", "qx1",
            "qpr0", "qpr1", "qks0", "qks1", "qvs")}

    def emit(self, eng, meth, *args, inc=None, inc_n=1, **kw):
        if inc is not None:
            self.cnt[inc] += inc_n
        self.ops[eng].append(("c", meth, args, kw, inc, inc_n))
        return self.cnt[inc] if inc is not None else None

    def dma(self, eng, out, in_, inc, **kw):
        return self.emit(eng, "dma_start", inc=inc, inc_n=16, out=out, in_=in_,
                         **kw)

    def wait(self, eng, sem, val):
        if val is None or val <= 0:
            return
        self.ops[eng].append(("w", sem, val))

    def check(self):
        for e in ENGINES:
            for op in self.ops[e]:
                if op[0] == "w":
                    assert op[2] <= self.cnt[op[1]], (
                        f"{e} waits {op[1]}>={op[2]} final {self.cnt[op[1]]}")

    def replay(self, name, eng, sems):
        last = {}
        for op in self.ops[name]:
            if op[0] == "w":
                _, sem, v = op
                if last.get(sem, -1) >= v:
                    continue
                last[sem] = v
                eng.wait_ge(sems[sem], v)
            else:
                _, meth, args, kw, inc, inc_n = op
                inst = getattr(eng, meth)(*args, **kw)
                if inc is not None:
                    inst.then_inc(sems[inc], inc_n)


def build(limit=None, dump=False):
    nc = bass.Bass("TRN2", target_bir_lowering=False)

    x_loc = nc.dram_tensor("x_loc", [B, SC, D], F32, kind="ExternalInput").ap()
    wq_sl = nc.dram_tensor("wq_sl", [768, D], F32, kind="ExternalInput").ap()
    wp_sl = nc.dram_tensor("wp_sl", [256, D], F32, kind="ExternalInput").ap()
    up_sl = nc.dram_tensor("up_sl", [1024, D], F32, kind="ExternalInput").ap()
    gt_sl = nc.dram_tensor("gt_sl", [1024, D], F32, kind="ExternalInput").ap()
    dn_sl = nc.dram_tensor("dn_sl", [256, DI], F32, kind="ExternalInput").ap()
    v_suug = nc.dram_tensor("v_suug", [1, D], F32, kind="ExternalInput").ap()
    v_sudp = nc.dram_tensor("v_sudp", [1, DI], F32, kind="ExternalInput").ap()
    v_wsug = nc.dram_tensor("v_wsug", [1, 2 * DI], F32, kind="ExternalInput").ap()
    v_wsdp = nc.dram_tensor("v_wsdp", [1, D], F32, kind="ExternalInput").ap()
    v_bpr = nc.dram_tensor("v_bpr", [1, D], F32, kind="ExternalInput").ap()
    c_h128d = nc.dram_tensor("c_h128", [128, 128], BF16, kind="ExternalInput").ap()
    c_idbd = nc.dram_tensor("c_idb", [128, 128], BF16, kind="ExternalInput").ap()
    c_idfd = nc.dram_tensor("c_idf", [128, 128], F32, kind="ExternalInput").ap()
    c_onesd = nc.dram_tensor("c_onesb", [128, 128], BF16, kind="ExternalInput").ap()
    out_loc = nc.dram_tensor("out_loc", [B, SC, D], F32, kind="ExternalOutput").ap()

    sh_attn = nc.dram_tensor("sh_attn", [1024, D], BF16, kind="Internal").ap()
    g_attn = nc.dram_tensor("g_attn", [8192, D], BF16, kind="Internal").ap()
    sh_ffn = nc.dram_tensor("sh_ffn", [3072, D], BF16, kind="Internal").ap()
    g_ffn = nc.dram_tensor("g_ffn", [24576, D], BF16, kind="Internal").ap()
    kv_sh = nc.dram_tensor("kv_sh", [8192, 1040], BF16, kind="Internal").ap()
    g_kv_lo = nc.dram_tensor("g_kv_lo", [32768, 1040], BF16, kind="Internal").ap()
    g_kv_hi = nc.dram_tensor("g_kv_hi", [32768, 1040], BF16, kind="Internal").ap()
    if dump:
        d_big1 = nc.dram_tensor("d_big1", [128, 65536], BF16,
                                kind="ExternalOutput").ap()
        d_big2 = nc.dram_tensor("d_big2", [128, 8192], BF16,
                                kind="ExternalOutput").ap()
        d_gattn = nc.dram_tensor("d_gattn", [8192, D], BF16,
                                 kind="ExternalOutput").ap()
        d_kvsh = nc.dram_tensor("d_kvsh", [8192, 1040], BF16,
                                kind="ExternalOutput").ap()
        d_gffn = nc.dram_tensor("d_gffn", [24576, D], BF16,
                                kind="ExternalOutput").ap()

    k_lo = g_kv_lo[:, 0:512].rearrange("(r g d) s -> g d r s", r=8, g=32)
    v_lo = g_kv_lo[:, 512:1040].rearrange("(r g k) c -> g k r c", r=8, g=32)
    k_hi = g_kv_hi[:, 0:512].rearrange("(r g d) s -> g d r s", r=8, g=32)
    v_hi = g_kv_hi[:, 512:1040].rearrange("(r g k) c -> g k r c", r=8, g=32)

    def k_view(g):
        return k_lo[g] if g < 32 else k_hi[g - 32]

    def v_view(g):
        return v_lo[g] if g < 32 else v_hi[g - 32]

    with ExitStack() as ctx:
        sbt = lambda name, shape, dt: ctx.enter_context(nc.sbuf_tensor(name, shape, dt))
        pst = lambda name, shape, dt: ctx.enter_context(nc.psum_tensor(name, shape, dt))

        c_h128 = sbt("c_h128s", [128, 128], BF16)
        c_idb = sbt("c_idbs", [128, 128], BF16)
        c_idf = sbt("c_idfs", [128, 128], F32)
        c_ones = sbt("c_oness", [128, 128], BF16)
        vecs = sbt("vecs", [1, 1024], F32)
        col_su = sbt("col_su", [128, 16], F32)
        col_bp = sbt("col_bp", [128, 16], F32)
        col_wsu = sbt("col_wsu", [128, 64], F32)
        col_wsg = sbt("col_wsg", [128, 64], F32)
        col_wsd = sbt("col_wsd", [128, 16], F32)
        col_sdp = sbt("col_sdp", [128, 64], F32)
        row_a = sbt("row_a", [1, 512], F32)
        row_b = sbt("row_b", [1, 512], F32)
        row_c = sbt("row_c", [1, 512], F32)
        row_bf = sbt("row_bf", [1, 1024], BF16)
        eps1 = sbt("eps1", [1, 1], F32)
        mb = sbt("mb", [128, 512], BF16)
        rb = sbt("rb", [128, 512], BF16)
        bca = sbt("bca", [128, 512], F32)
        sq = sbt("sq", [128, 1024], BF16)
        sgb = sbt("sgb", [128, 1024], BF16)
        kstg = sbt("kstg", [128, 1024], BF16)
        vtmp = sbt("vtmp", [128, 1024], BF16)
        vstg = sbt("vstg", [128, 528], BF16)
        stg_out = sbt("stg_out", [128, 2048], BF16)
        xbf = sbt("xbf", [128, 4096], BF16)
        strips = sbt("strips", [128, 12288], BF16)
        big1 = sbt("big1", [128, 65536], BF16)
        big2 = sbt("big2", [128, 8192], BF16)

        pf = [pst(f"pf{i}", [128, 512], F32) for i in range(6)]
        pt = [pst(f"pt{i}", [128, 1024], BF16) for i in range(2)]

        sems = {}
        for sname in ("qs", "qa", "qp", "cc", "pe", "dv", "ac",
                      "qk0", "qk1", "qv0", "qv1", "qw0", "qw1",
                      "qu0", "qu1", "qu2", "qg0", "qg1", "qg2",
                      "qd0", "qd1", "qd2", "qd3",
                      "qx0", "qx1", "qpr0", "qpr1",
                      "qks0", "qks1", "qvs"):
            sems[sname] = ctx.enter_context(nc.semaphore(sname))

        def strip_slot(i):
            return strips[:, i * 2048:(i + 1) * 2048]

        def U(g):
            return big1[:, g * 512:(g + 1) * 512]

        HB = 32768

        def xln(fb, tb=None):
            a = big1[:, HB + fb * 2048: HB + (fb + 1) * 2048]
            return a if tb is None else a[:, tb * 512:(tb + 1) * 512]

        r1 = xln

        def KB(p):
            return big1[:, HB + p * 4096: HB + (p + 1) * 4096]

        def VB(p):
            return big1[:, HB + 8192 + p * 4224: HB + 8192 + (p + 1) * 4224]

        def PT4(i):
            base = HB + 16640
            return big1[:, base + i * 512: base + (i + 1) * 512]

        def big2c(cidx, tt=None):
            a = big2[:, cidx * 512:(cidx + 1) * 512]
            return a if tt is None else a[:, tt * 128:(tt + 1) * 128]

        sch = Sched()
        C = sch.cnt

        def _phase_end(name):
            if limit == name:
                sch.wait("sync", "qs", C["qs"])
                sch.wait("sync", "qa", C["qa"])
                sch.wait("sync", "qp", C["qp"])
                sch.wait("sync", "qk0", C["qk0"])
                sch.wait("sync", "qk1", C["qk1"])
                sch.wait("sync", "qv0", C["qv0"])
                sch.wait("sync", "qv1", C["qv1"])
                sch.wait("sync", "qw0", C["qw0"])
                sch.wait("sync", "qw1", C["qw1"])
                sch.wait("sync", "qu0", C["qu0"])
                sch.wait("sync", "qu1", C["qu1"])
                sch.wait("sync", "qu2", C["qu2"])
                sch.wait("sync", "qg2", C["qg2"])
                sch.wait("sync", "qg0", C["qg0"])
                sch.wait("sync", "qg1", C["qg1"])
                sch.wait("sync", "qd0", C["qd0"])
                sch.wait("sync", "qd1", C["qd1"])
                sch.wait("sync", "qd2", C["qd2"])
                sch.wait("sync", "qd3", C["qd3"])
                sch.wait("sync", "qx0", C["qx0"])
                sch.wait("sync", "qx1", C["qx1"])
                sch.wait("sync", "qpr0", C["qpr0"])
                sch.wait("sync", "qpr1", C["qpr1"])
                sch.wait("sync", "qks0", C["qks0"])
                sch.wait("sync", "qks1", C["qks1"])
                sch.wait("sync", "qvs", C["qvs"])
                sch.wait("gpsimd", "qp", C["qp"])
                return True
            return False

        sch.dma("sync", c_h128[:], c_h128d[:, :], "qs")
        sch.dma("sync", c_idb[:], c_idbd[:, :], "qs")
        sch.dma("sync", c_idf[:], c_idfd[:, :], "qs")
        sch.dma("sync", c_ones[:], c_onesd[:, :], "qs")
        c0_qs = C["qs"]
        sch.emit("vector", "memset", vstg[:], 1.0, inc="dv")
        sch.emit("vector", "memset", eps1[:], EPS, inc="dv")

        colplan = [
            (v_suug, [(col_su, 16, 1.0)]),
            (v_bpr, [(col_bp, 16, 1.0)]),
            (v_wsdp, [(col_wsd, 16, ISI)]),
            (v_sudp, [(col_sdp, 64, 1.0)]),
            (v_wsug, [(col_wsu, 64, ISD), (col_wsg, 64, ISD)]),
        ]
        tpi = 0
        for src, dests in colplan:
            flat = []
            for dst, n, scl in dests:
                for j in range(n):
                    flat.append((dst, j, scl))
            total = len(flat)
            for ch in range((total + 7) // 8):
                ncol = min(8, total - ch * 8)
                sch.wait("sync", "dv", C["dv"])
                sch.dma("sync", vecs[0:1, 0:ncol * 128],
                        src[0:1, ch * 1024: ch * 1024 + ncol * 128], "qs")
                qs_here = C["qs"]
                sch.wait("tensor", "qs", qs_here)
                for j in range(ncol):
                    dst, colj, scl = flat[ch * 8 + j]
                    bank = pf[tpi % 2]
                    sch.wait("tensor", "dv", C["dv"])
                    sch.emit("tensor", "transpose", bank[:, 0:1],
                             vecs[0:1, j * 128:(j + 1) * 128], c_idf[0:1, 0:1],
                             inc="pe")
                    sch.wait("vector", "pe", C["pe"])
                    sch.emit("vector", "tensor_scalar", dst[:, colj:colj + 1],
                             bank[:, 0:1], scl, None, ALU.mult, inc="dv")
                    tpi += 1
        for ub in range(NU):
            sch.emit("vector", "tensor_mul", col_wsu[:, ub:ub + 1],
                     col_wsu[:, ub:ub + 1], col_sdp[:, ub:ub + 1], inc="dv")

        prep_state = {
            "xb_pe": {0: 0, 1: 0},
            "reg_qs": {0: 0, 1: 0},
            "pt_dv": {0: 0, 1: 0},
            "idx": 0,
        }

        def prep_slab(slab_ap, fcols, dst_ap, stages):
            idx = prep_state["idx"]
            prep_state["idx"] += 1
            if fcols <= 2048:
                reg = idx % 2
                s2 = strips[:, reg * 4096: reg * 4096 + fcols]
                s3 = strips[:, reg * 4096 + 2048: reg * 4096 + 2048 + fcols]
            else:
                reg = 0
                s2 = strips[:, 4096:4096 + fcols]
                s3 = None
            nchunk = (fcols + 2047) // 2048
            for q in range(nchunk):
                w = min(2048, fcols - q * 2048)
                xslot = (idx + q) % 2
                xb = xbf[:, xslot * 2048: xslot * 2048 + w]
                sch.wait("gpsimd", "pe", prep_state["xb_pe"][xslot])
                sch.dma("gpsimd", xb, slab_ap[:, q * 2048: q * 2048 + w],
                        f"qx{xslot}")
                sch.wait("tensor", f"qx{xslot}", C[f"qx{xslot}"])
                for half in range(w // 1024):
                    bank = pt[half % 2]
                    sch.wait("tensor", "dv", prep_state["pt_dv"][half % 2])
                    for t in range(8):
                        ti = half * 8 + t
                        sch.emit("tensor", "transpose", bank[:, t * 128:(t + 1) * 128],
                                 xb[:, ti * 128:(ti + 1) * 128], c_idb[:, :],
                                 inc="pe" if t == 7 else None)
                    sch.wait("vector", "pe", C["pe"])
                    if q == 0 and half == 0:
                        if fcols <= 2048:
                            sch.wait("vector", f"qpr{reg}",
                                     prep_state["reg_qs"][reg])
                        else:
                            sch.wait("vector", "qpr0", C["qpr0"])
                            sch.wait("vector", "qpr1", C["qpr1"])
                    sch.emit("vector", "tensor_copy",
                             s2[:, q * 2048 + half * 1024: q * 2048 + half * 1024 + 1024],
                             bank[:, 0:1024], inc="dv")
                    prep_state["pt_dv"][half % 2] = C["dv"]
                prep_state["xb_pe"][xslot] = C["pe"]
            if stages:
                if fcols <= 2048:
                    cur, oth = s2, s3
                    for st in range(stages):
                        half = 2 ** st
                        va = cur.rearrange("p (q t r) -> p q t r", t=2,
                                           r=half * 128)
                        vo = oth.rearrange("p (q t r) -> p q t r", t=2,
                                           r=half * 128)
                        sch.emit("vector", "tensor_add", vo[:, :, 0, :],
                                 va[:, :, 0, :], va[:, :, 1, :], inc="dv")
                        sch.emit("vector", "tensor_sub", vo[:, :, 1, :],
                                 va[:, :, 0, :], va[:, :, 1, :], inc="dv")
                        cur, oth = oth, cur
                    src_final = cur
                else:
                    for st in range(stages):
                        half = 2 ** st
                        va = s2.rearrange("p (q t r) -> p q t r", t=2,
                                          r=half * 128)
                        nq = fcols // (2 * half * 128)
                        span = half * 128
                        for qq in range(nq):
                            for cc0 in range(0, span, 1024):
                                w2 = min(1024, span - cc0)
                                a = va[:, qq, 0, cc0:cc0 + w2]
                                bb = va[:, qq, 1, cc0:cc0 + w2]
                                t2 = kstg[:, 0:w2]
                                sch.emit("vector", "tensor_copy", t2, a)
                                sch.emit("vector", "tensor_add", a, a, bb)
                                sch.emit("vector", "tensor_sub", bb, t2, bb,
                                         inc="dv")
                    src_final = s2
            else:
                src_final = s2
            sch.wait("sync", "dv", C["dv"])
            sch.dma("sync", dst_ap, src_final, f"qpr{reg}")
            prep_state["reg_qs"][reg] = C[f"qpr{reg}"]

        for cb in range(6):
            prep_slab(wq_sl[cb * 128:(cb + 1) * 128, :], D,
                      sh_attn[cb * 128:(cb + 1) * 128, :], 0)
        for cb in range(2):
            prep_slab(wp_sl[cb * 128:(cb + 1) * 128, :], D,
                      sh_attn[768 + cb * 128: 768 + (cb + 1) * 128, :], 0)
        sch.wait("gpsimd", "qpr0", C["qpr0"])
        sch.wait("gpsimd", "qpr1", C["qpr1"])
        sch.emit("gpsimd", "collective_compute", "AllGather", ALU.bypass,
                 replica_groups=RG, ins=[sh_attn[:, :]], outs=[g_attn[:, :]],
                 inc="cc")
        cc_attn = C["cc"]
        stop = _phase_end("p1")

        def emit_ln(srcs, dsts, su_fold):
            sch.wait("tensor", "dv", C["dv"])
            sumsq_pe = {}
            for fb in range(NF):
                if fb >= 2:
                    sch.wait("vector", "pe", sumsq_pe[fb - 2])
                sch.emit("vector", "tensor_mul",
                         sq[:, (fb % 2) * 512:(fb % 2) * 512 + 512],
                         srcs[fb], srcs[fb], inc="dv")
                sch.wait("tensor", "dv", C["dv"])
                sch.emit("tensor", "matmul", pf[4][0:1, :], c_ones[:, 0:1],
                         srcs[fb], start=(fb == 0), stop=(fb == NF - 1))
                sch.emit("tensor", "matmul", pf[5][0:1, :], c_ones[:, 0:1],
                         sq[:, (fb % 2) * 512:(fb % 2) * 512 + 512],
                         start=(fb == 0), stop=(fb == NF - 1), inc="pe")
                sumsq_pe[fb] = C["pe"]
            pe_sums = C["pe"]
            sch.wait("vector", "pe", pe_sums)
            sch.emit("vector", "tensor_scalar", row_a[0:1, :], pf[4][0:1, :],
                     1.0 / D, None, ALU.mult)
            sch.emit("vector", "tensor_copy", row_bf[0:1, 0:512], row_a[0:1, :])
            sch.emit("vector", "tensor_mul", row_b[0:1, :], row_a[0:1, :],
                     row_a[0:1, :])
            sch.emit("vector", "scalar_tensor_tensor", row_b[0:1, :],
                     pf[5][0:1, :], 1.0 / D, row_b[0:1, :], ALU.mult,
                     ALU.subtract, inc="dv")
            sch.wait("scalar", "dv", C["dv"])
            sch.emit("scalar", "activation", row_c[0:1, :], row_b[0:1, :],
                     AF.Sqrt, bias=eps1[0:1, 0:1], inc="ac")
            sch.wait("vector", "ac", C["ac"])
            sch.emit("vector", "reciprocal", row_a[0:1, :], row_c[0:1, :])
            sch.emit("vector", "tensor_copy", row_bf[0:1, 512:1024],
                     row_a[0:1, :], inc="dv")
            sch.wait("tensor", "dv", C["dv"])
            sch.emit("tensor", "matmul", pf[4][:, :], c_ones[0:1, :],
                     row_bf[0:1, 0:512], start=True, stop=True, inc="pe")
            sch.wait("vector", "pe", C["pe"])
            sch.emit("vector", "tensor_copy", mb[:, :], pf[4][:, :], inc="dv")
            sch.wait("tensor", "dv", C["dv"])
            sch.emit("tensor", "matmul", pf[4][:, :], c_ones[0:1, :],
                     row_bf[0:1, 512:1024], start=True, stop=True, inc="pe")
            sch.wait("vector", "pe", C["pe"])
            sch.emit("vector", "tensor_copy", rb[:, :], pf[4][:, :], inc="dv")
            for fb in range(NF):
                sch.emit("vector", "tensor_sub", dsts[fb], srcs[fb], mb[:, :])
                if su_fold is not None:
                    sch.emit("vector", "scalar_tensor_tensor", dsts[fb],
                             dsts[fb], su_fold(fb), rb[:, :], ALU.mult,
                             ALU.mult, inc="dv")
                else:
                    sch.emit("vector", "tensor_mul", dsts[fb], dsts[fb],
                             rb[:, :], inc="dv")

        for tb in range(B):
            sch.wait("vector", "pe", C["pe"])
            for tt in range(4):
                par = tt % 2
                xb = xbf[:, par * 2048:(par + 1) * 2048]
                sch.wait("gpsimd", "dv", C["dv"])
                sch.dma("gpsimd", xb, x_loc[tb, tt * 128:(tt + 1) * 128, :],
                        f"qx{par}")
                sch.wait("tensor", f"qx{par}", C[f"qx{par}"])
                b2v = big2.rearrange("p (f q) -> p f q", q=512)
                for half in range(2):
                    bank = pt[half]
                    sch.wait("tensor", "dv", C["dv"])
                    for t in range(8):
                        fb = half * 8 + t
                        sch.emit("tensor", "transpose",
                                 bank[:, t * 128:(t + 1) * 128],
                                 xb[:, fb * 128:(fb + 1) * 128], c_idb[:, :],
                                 inc="pe" if t == 7 else None)
                    sch.wait("vector", "pe", C["pe"])
                    dstv = b2v[:, half * 8:(half + 1) * 8,
                               tt * 128:(tt + 1) * 128]
                    srcv = bank.rearrange("p (t c) -> p t c", c=128)
                    sch.emit("vector", "tensor_copy", dstv, srcv, inc="dv")
            emit_ln([big2c(fb) for fb in range(NF)],
                    [xln(fb, tb) for fb in range(NF)], None)
        stop = stop or _phase_end("p2")

        group_pe = {}
        evac_dv = {}
        kq = {0: 0, 1: 0}
        vq = {0: 0}
        vt_pe = {0: 0, 1: 0}
        kwr = 0
        vwr = 0
        cc_kv = {}
        sch.wait("tensor", "dv", C["dv"])
        for hpass in range(2):
            tbs = (2 * hpass, 2 * hpass + 1)
            for cb in range(48):
                h, pp = cb // 3, cb % 3
                sp = cb % 2
                r0 = (cb // 6) * 1024 + (cb % 6) * 128
                key = (hpass, cb)
                sch.wait("sync", "pe", group_pe.get((hpass, cb - 2),
                                                    group_pe.get((hpass - 1, 47), 0)))
                if (hpass, cb) == (0, 0):
                    sch.wait("sync", "cc", cc_attn)
                sch.dma("sync", strip_slot(sp), g_attn[r0:r0 + 128, :],
                        f"qw{sp}")
                sch.wait("tensor", f"qw{sp}", C[f"qw{sp}"])
                for tb in tbs:
                    sch.wait("tensor", "dv",
                             evac_dv.get((hpass, cb - 2, tb), 0))
                for fb in range(NF):
                    for tb in tbs:
                        bank = pf[(cb % 2) * 2 + (tb % 2)]
                        sch.emit("tensor", "matmul", bank[:, :],
                                 strip_slot(sp)[:, fb * 128:(fb + 1) * 128],
                                 xln(fb, tb), start=(fb == 0),
                                 stop=(fb == NF - 1),
                                 inc="pe" if fb == NF - 1 else None)
                group_pe[key] = C["pe"]
                for tb in tbs:
                    bank = pf[(cb % 2) * 2 + (tb % 2)]
                    g = tb * 16 + h
                    sch.wait("vector", "pe", group_pe[key])
                    if pp == 0:
                        sch.emit("vector", "tensor_copy", U(g), bank[:, :],
                                 inc="dv")
                    elif pp == 1:
                        par = kwr % 2
                        ks = kstg[:, par * 512:(par + 1) * 512]
                        sch.wait("vector", f"qks{par}", kq[par])
                        sch.emit("vector", "tensor_copy", ks, bank[:, :],
                                 inc="dv")
                        sch.wait("gpsimd", "dv", C["dv"])
                        sch.dma("gpsimd", kv_sh[g * 128:(g + 1) * 128, 0:512],
                                ks, f"qks{par}")
                        kq[par] = C[f"qks{par}"]
                        kwr += 1
                    else:
                        par = vwr % 2
                        vt = vtmp[:, par * 512:(par + 1) * 512]
                        vs = vstg[:, 0:528]
                        sch.wait("vector", "pe", vt_pe[par])
                        sch.emit("vector", "tensor_copy", vt, bank[:, :],
                                 inc="dv")
                        dv_vt = C["dv"]
                        tbank = pt[vwr % 2]
                        sch.wait("tensor", "dv", dv_vt)
                        for sh in range(4):
                            sch.emit("tensor", "transpose",
                                     tbank[:, sh * 128:(sh + 1) * 128],
                                     vt[:, sh * 128:(sh + 1) * 128],
                                     c_idb[:, :],
                                     inc="pe" if sh == 3 else None)
                        sch.wait("vector", "pe", C["pe"])
                        sch.wait("vector", "qvs", vq[0])
                        vsv = vstg.rearrange("p (s c) -> p s c", c=132)
                        sch.emit("vector", "tensor_copy", vsv[:, :, 0:128],
                                 tbank.rearrange("p (t c) -> p t c",
                                                 c=128)[:, 0:4, :],
                                 inc="dv")
                        vt_pe[par] = C["pe"]
                        sch.wait("gpsimd", "dv", C["dv"])
                        sch.dma("gpsimd",
                                kv_sh[g * 128:(g + 1) * 128, 512:1040],
                                vs, "qvs")
                        vq[0] = C["qvs"]
                        vwr += 1
                    evac_dv[(hpass, cb, tb)] = C["dv"]
            if not stop:
                sch.wait("gpsimd", "qks0", C["qks0"])
                sch.wait("gpsimd", "qks1", C["qks1"])
                sch.wait("gpsimd", "qvs", C["qvs"])
                gdst = g_kv_lo if hpass == 0 else g_kv_hi
                gsrc = kv_sh[hpass * 4096:(hpass + 1) * 4096, :]
                sch.emit("gpsimd", "collective_compute", "AllGather",
                         ALU.bypass, replica_groups=RG, ins=[gsrc],
                         outs=[gdst[:, :]], inc="cc")
                cc_kv[hpass] = C["cc"]
        stop = stop or _phase_end("p3")

        for ub in range(8 if not stop else 0):
            prep_slab(up_sl[ub * 128:(ub + 1) * 128, :], D,
                      sh_ffn[ub * 128:(ub + 1) * 128, :], 4)
        for ub in range(8 if not stop else 0):
            prep_slab(gt_sl[ub * 128:(ub + 1) * 128, :], D,
                      sh_ffn[1024 + ub * 128: 1024 + (ub + 1) * 128, :], 4)
        for ob in range(2 if not stop else 0):
            dn_dst = sh_ffn[2048 + ob * 512: 2048 + (ob + 1) * 512,
                            :].rearrange("(fi q) c -> fi (q c)", q=4)
            prep_slab(dn_sl[ob * 128:(ob + 1) * 128, :], DI, dn_dst, 6)
        if not stop:
            sch.wait("gpsimd", "qpr0", C["qpr0"])
            sch.wait("gpsimd", "qpr1", C["qpr1"])
            sch.emit("gpsimd", "collective_compute", "AllGather", ALU.bypass,
                     replica_groups=RG, ins=[sh_ffn[:, :]], outs=[g_ffn[:, :]],
                     inc="cc")
        cc_ffn = C["cc"]
        stop = stop or _phase_end("p1b")

        LAG = 2
        kload = {}
        vload = {}
        scores_pe = {}
        expcnt = {}
        den_pe = {}
        ctx_pe = {}
        recip_dv = {}
        mul_dv = {}
        bc_copy_dv = {}
        sch.wait("tensor", "dv", C["dv"])
        sch.wait("tensor", "ac", C["ac"])
        p3_pe_done = C["pe"]

        def emit_kv_load(g):
            p = g % 2
            if g >= 2:
                sch.wait("sync", "pe", scores_pe[(g - 2, NKT - 1)])
                sch.wait("scalar", "pe", ctx_pe[g - 2])
            else:
                sch.wait("sync", "pe", p3_pe_done)
                sch.wait("scalar", "pe", p3_pe_done)
            cw = cc_kv[0] if g < 32 else cc_kv[1]
            sch.wait("sync", "cc", cw)
            sch.wait("scalar", "cc", cw)
            sch.dma("sync", KB(p), k_view(g), f"qk{p}")
            kload[g] = C[f"qk{p}"]
            sch.dma("scalar", VB(p), v_view(g), f"qv{p}")
            vload[g] = C[f"qv{p}"]

        def emit_ctx_den(g, kt):
            p = g % 2
            sch.wait("tensor", "ac", expcnt[(g, kt)])
            if kt == 0:
                sch.wait("tensor", f"qv{p}", vload[g])
                sch.wait("tensor", "dv", mul_dv.get(g - 1, 0))
            sch.emit("tensor", "matmul", pf[3][:, :],
                     VB(p)[:, kt * 132: kt * 132 + 128], PT4(kt % 4),
                     start=(kt == 0), stop=(kt == NKT - 1),
                     inc="pe" if kt == NKT - 1 else None)
            if kt == NKT - 1:
                ctx_pe[g] = C["pe"]
            if kt == 0:
                sch.wait("tensor", "dv", recip_dv.get(g - 1, 0))
            sch.emit("tensor", "matmul", pf[4][0:1, :], c_ones[:, 0:1],
                     PT4(kt % 4), start=(kt == 0), stop=(kt == NKT - 1),
                     inc="pe")
            den_pe[(g, kt)] = C["pe"]

        if not stop:
            emit_kv_load(0)
            emit_kv_load(1)
        for g in range(NG if not stop else 0):
            p = g % 2
            sch.wait("tensor", f"qk{p}", kload[g])
            for kt in range(NKT):
                prev = (g, kt - 3) if kt >= 3 else (g - 1, NKT - 3 + kt)
                sch.wait("tensor", "ac", expcnt.get(prev, 0))
                sch.emit("tensor", "matmul", pf[kt % 3][:, :],
                         KB(p)[:, kt * 128:(kt + 1) * 128], U(g),
                         start=True, stop=True, inc="pe")
                scores_pe[(g, kt)] = C["pe"]
                sch.wait("scalar", "pe", scores_pe[(g, kt)])
                sch.emit("scalar", "activation", PT4(kt % 4), pf[kt % 3][:, :],
                         AF.Exp, scale=SCALE_QK, inc="ac")
                expcnt[(g, kt)] = C["ac"]
                if kt >= LAG:
                    emit_ctx_den(g, kt - LAG)
            for kt in range(NKT - LAG, NKT):
                emit_ctx_den(g, kt)
            sch.wait("vector", "pe", den_pe[(g, NKT - 1)])
            sch.emit("vector", "reciprocal", row_c[0:1, :], pf[4][0:1, :])
            sch.emit("vector", "tensor_copy", row_bf[0:1, 0:512],
                     row_c[0:1, :], inc="dv")
            recip_dv[g] = C["dv"]
            sch.wait("tensor", "dv", recip_dv[g])
            sch.wait("tensor", "dv", bc_copy_dv.get(g - 1, 0))
            sch.emit("tensor", "matmul", pf[5][:, :], c_ones[0:1, :],
                     row_bf[0:1, 0:512], start=True, stop=True, inc="pe")
            sch.wait("vector", "pe", C["pe"])
            sch.emit("vector", "tensor_copy", bca[:, :], pf[5][:, :],
                     inc="dv")
            bc_copy_dv[g] = C["dv"]
            sch.emit("vector", "tensor_mul", U(g), pf[3][:, :], bca[:, :],
                     inc="dv")
            mul_dv[g] = C["dv"]
            if g + 2 < NG:
                emit_kv_load(g + 2)

        stop = stop or _phase_end("p4")
        op_group_pe = {}
        op_evac_dv = {}
        sch.wait("tensor", "dv", C["dv"])
        sch.wait("tensor", "ac", C["ac"])
        for bp in range(B if not stop else 0):
            sch.wait("sync", "pe", C["pe"])
            sch.wait("vector", "pe", C["pe"])
            for tt in range(4):
                par = tt % 2
                xb = xbf[:, par * 2048:(par + 1) * 2048]
                sch.wait("gpsimd", "dv", C["dv"])
                sch.dma("gpsimd", xb, x_loc[bp, tt * 128:(tt + 1) * 128, :],
                        f"qx{par}")
                sch.wait("tensor", f"qx{par}", C[f"qx{par}"])
                b2v = big2.rearrange("p (f q) -> p f q", q=512)
                for half in range(2):
                    bank = pt[half]
                    sch.wait("tensor", "dv", C["dv"])
                    for t in range(8):
                        fb = half * 8 + t
                        sch.emit("tensor", "transpose",
                                 bank[:, t * 128:(t + 1) * 128],
                                 xb[:, fb * 128:(fb + 1) * 128], c_idb[:, :],
                                 inc="pe" if t == 7 else None)
                    sch.wait("vector", "pe", C["pe"])
                    dstv = b2v[:, half * 8:(half + 1) * 8,
                               tt * 128:(tt + 1) * 128]
                    srcv = bank.rearrange("p (t c) -> p t c", c=128)
                    sch.emit("vector", "tensor_copy", dstv, srcv, inc="dv")
            for cb in range(NF):
                sp = cb % 2
                r0 = (cb // 2) * 1024 + 768 + (cb % 2) * 128
                sch.wait("sync", "pe", op_group_pe.get((bp, cb - 2), 0))
                sch.dma("sync", strip_slot(sp), g_attn[r0:r0 + 128, :],
                        f"qw{sp}")
                sch.wait("tensor", f"qw{sp}", C[f"qw{sp}"])
                sch.wait("tensor", "dv", op_evac_dv.get((bp, cb - 4), 0))
                for j in range(NF):
                    sch.emit("tensor", "matmul", pf[cb % 4][:, :],
                             strip_slot(sp)[:, j * 128:(j + 1) * 128],
                             U(4 * j + bp), start=(j == 0),
                             stop=(j == NF - 1),
                             inc="pe" if j == NF - 1 else None)
                op_group_pe[(bp, cb)] = C["pe"]
                sch.wait("vector", "pe", C["pe"])
                sch.emit("vector", "scalar_tensor_tensor", r1(cb, bp),
                         pf[cb % 4][:, :], col_bp[:, cb:cb + 1], big2c(cb),
                         ALU.add, ALU.add, inc="dv")
                op_evac_dv[(bp, cb)] = C["dv"]

        z_ac = {}
        ug_pe = {}
        silu_ac = {}
        y_dv = {}
        z2_ac = {}
        dn_pe = {}
        dn_chunk_pe = {}
        out_dv = {}
        out_qp = {}
        stop = stop or _phase_end("p5")
        if not stop:
            sch.wait("sync", "cc", cc_ffn)
            sch.wait("scalar", "cc", cc_ffn)
        for tb in range(B if not stop else 0):
            sch.wait("tensor", "dv", C["dv"])
            sch.wait("tensor", "ac", C["ac"])
            sch.wait("sync", "pe", C["pe"])
            sch.wait("scalar", "pe", C["pe"])
            sch.wait("vector", "pe", C["pe"])
            emit_ln([r1(fb, tb) for fb in range(NF)],
                    [big2c(fb) for fb in range(NF)],
                    lambda fb: col_su[:, fb:fb + 1])
            for cix in range(NF):
                bank = pf[cix % 4]
                sch.wait("tensor", "dv", z_ac.get((tb, cix - 4), C["dv"] if cix == 0 else 0))
                sch.emit("tensor", "matmul", bank[:, :], c_h128[:, :],
                         big2c(cix), start=True, stop=True, inc="pe")
                sch.wait("vector", "pe", C["pe"])
                sch.emit("vector", "tensor_copy", big2c(cix), bank[:, :],
                         inc="dv")
                z_ac[(tb, cix)] = C["dv"]
            z_done = C["dv"]
            for ub in range(NU):
                r = ub // 8
                ru = r * 3072 + (ub % 8) * 128
                rg = r * 3072 + 1024 + (ub % 8) * 128
                sch.wait("sync", "pe", ug_pe.get((tb, ub - 3), 0))
                sch.dma("sync", strip_slot(ub % 3), g_ffn[ru:ru + 128, :],
                        f"qu{ub % 3}")
                up_qs = C[f"qu{ub % 3}"]
                sch.wait("scalar", "pe", ug_pe.get((tb, ub - 3), 0))
                sch.dma("scalar", strip_slot(3 + ub % 3),
                        g_ffn[rg:rg + 128, :], f"qg{ub % 3}")
                gt_qa = C[f"qg{ub % 3}"]
                if ub == 0:
                    sch.wait("tensor", "dv", z_done)
                sch.wait("tensor", f"qu{ub % 3}", up_qs)
                sch.wait("tensor", f"qg{ub % 3}", gt_qa)
                sch.wait("tensor", "dv", y_dv.get((tb, ub - 2), 0))
                sch.wait("tensor", "ac", silu_ac.get((tb, ub - 2), 0))
                bu = pf[ub % 2]
                bg = pf[2 + ub % 2]
                for fb in range(NF):
                    sch.emit("tensor", "matmul", bu[:, :],
                             strip_slot(ub % 3)[:, fb * 128:(fb + 1) * 128],
                             big2c(fb), start=(fb == 0), stop=(fb == NF - 1),
                             inc="pe" if fb == NF - 1 else None)
                for fb in range(NF):
                    sch.emit("tensor", "matmul", bg[:, :],
                             strip_slot(3 + ub % 3)[:, fb * 128:(fb + 1) * 128],
                             big2c(fb), start=(fb == 0), stop=(fb == NF - 1),
                             inc="pe" if fb == NF - 1 else None)
                ug_pe[(tb, ub)] = C["pe"]
                sch.wait("scalar", "pe", C["pe"])
                sch.wait("scalar", "dv", y_dv.get((tb, ub - 2), 0))
                sg = sgb[:, (ub % 2) * 512:(ub % 2) * 512 + 512]
                sch.emit("scalar", "activation", sg, bg[:, :], AF.Silu,
                         scale=col_wsg[:, ub:ub + 1], inc="ac")
                silu_ac[(tb, ub)] = C["ac"]
                sch.wait("vector", "ac", C["ac"])
                sch.emit("vector", "scalar_tensor_tensor",
                         big1[:, ub * 512:(ub + 1) * 512], bu[:, :],
                         col_wsu[:, ub:ub + 1], sg, ALU.mult, ALU.mult,
                         inc="dv")
                y_dv[(tb, ub)] = C["dv"]
            for cix in range(NU):
                bank = pf[2 + cix % 4]
                if cix == 0:
                    sch.wait("tensor", "ac", silu_ac[(tb, NU - 1)])
                sch.wait("tensor", "dv", y_dv[(tb, cix)])
                sch.wait("tensor", "dv", z2_ac.get((tb, cix - 4), 0))
                sch.emit("tensor", "matmul", bank[:, :], c_h128[:, :],
                         big1[:, cix * 512:(cix + 1) * 512], start=True,
                         stop=True, inc="pe")
                sch.wait("vector", "pe", C["pe"])
                sch.emit("vector", "tensor_copy",
                         big1[:, cix * 512:(cix + 1) * 512], bank[:, :],
                         inc="dv")
                z2_ac[(tb, cix)] = C["dv"]
            z2_done = C["dv"]
            for ob in range(NF):
                r = ob // 2
                base = r * 3072 + 2048 + (ob % 2) * 512
                dview = g_ffn[base:base + 512, :].rearrange(
                    "(fi q) c -> q fi c", q=4)
                chunk_qs = []
                for q in range(4):
                    if ob > 0:
                        sch.wait("sync", "pe", dn_chunk_pe[(tb, ob - 1, q)])
                    else:
                        sch.wait("sync", "pe", ug_pe[(tb, NU - 1)])
                    sch.dma("sync", strip_slot(q), dview[q], f"qd{q}")
                    chunk_qs.append(C[f"qd{q}"])
                bank = pf[ob % 2]
                if ob == 0:
                    sch.wait("tensor", "dv", y_dv[(tb, NU - 1)])
                    sch.wait("tensor", "dv", z2_done)
                sch.wait("tensor", "dv", out_dv.get((tb, ob - 2), 0))
                for fb in range(NU):
                    q = fb // 16
                    if fb % 16 == 0:
                        sch.wait("tensor", f"qd{q}", chunk_qs[q])
                    sch.emit("tensor", "matmul", bank[:, :],
                             strip_slot(q)[:, (fb % 16) * 128:
                                           (fb % 16) * 128 + 128],
                             big1[:, fb * 512:(fb + 1) * 512],
                             start=(fb == 0), stop=(fb == NU - 1), inc="pe")
                    if fb % 16 == 15:
                        dn_chunk_pe[(tb, ob, q)] = C["pe"]
                dn_pe[(tb, ob)] = C["pe"]
                sch.wait("vector", "pe", C["pe"])
                sch.emit("vector", "scalar_tensor_tensor", big2c(ob),
                         bank[:, :], col_wsd[:, ob:ob + 1], r1(ob, tb),
                         ALU.mult, ALU.add, inc="dv")
                out_dv[(tb, ob)] = C["dv"]
            for tt in range(4):
                stg = stg_out[:, 0:2048]
                for half in range(2):
                    bank = pt[half]
                    sch.wait("tensor", "dv", C["dv"])
                    for t in range(8):
                        ob = half * 8 + t
                        sch.emit("tensor", "transpose",
                                 bank[:, t * 128:(t + 1) * 128],
                                 big2c(ob, tt), c_idb[:, :],
                                 inc="pe" if t == 7 else None)
                    sch.wait("vector", "pe", C["pe"])
                    if half == 0:
                        sch.wait("vector", "qp",
                                 out_qp.get((tb * 4 + tt) - 1, 0))
                    sch.emit("vector", "tensor_copy",
                             stg[:, half * 1024:(half + 1) * 1024],
                             bank[:, 0:1024], inc="dv")
                sch.wait("gpsimd", "dv", C["dv"])
                sch.dma("gpsimd", out_loc[tb, tt * 128:(tt + 1) * 128, :],
                        stg, "qp")
                out_qp[tb * 4 + tt] = C["qp"]

        sch.wait("sync", "qp", C["qp"])
        sch.wait("sync", "qs", C["qs"])
        sch.wait("sync", "qa", C["qa"])
        sch.wait("sync", "qk0", C["qk0"])
        sch.wait("sync", "qk1", C["qk1"])
        sch.wait("sync", "qv0", C["qv0"])
        sch.wait("sync", "qv1", C["qv1"])
        sch.wait("sync", "qw0", C["qw0"])
        sch.wait("sync", "qw1", C["qw1"])
        sch.wait("sync", "qu0", C["qu0"])
        sch.wait("sync", "qu1", C["qu1"])
        sch.wait("sync", "qg0", C["qg0"])
        sch.wait("sync", "qg1", C["qg1"])
        sch.wait("sync", "qd0", C["qd0"])
        sch.wait("sync", "qd1", C["qd1"])
        sch.wait("sync", "qd2", C["qd2"])
        sch.wait("sync", "qd3", C["qd3"])
        sch.wait("sync", "qx0", C["qx0"])
        sch.wait("sync", "qx1", C["qx1"])
        sch.wait("sync", "qpr0", C["qpr0"])
        sch.wait("sync", "qpr1", C["qpr1"])
        sch.wait("sync", "qks0", C["qks0"])
        sch.wait("sync", "qks1", C["qks1"])
        sch.wait("sync", "qvs", C["qvs"])
        sch.wait("gpsimd", "qp", C["qp"])
        if dump:
            sch.wait("sync", "pe", C["pe"])
            sch.wait("sync", "dv", C["dv"])
            sch.wait("sync", "ac", C["ac"])
            sch.wait("sync", "cc", C["cc"])
            sch.dma("sync", d_big1[:, :], big1[:], "qs")
            sch.dma("sync", d_big2[:, :], big2[:], "qs")
            sch.dma("sync", d_gattn[:, :], g_attn[:, :], "qs")
            sch.dma("sync", d_kvsh[:, :], kv_sh[:, :], "qs")
            sch.dma("sync", d_gffn[:, :], g_ffn[:, :], "qs")
            sch.wait("sync", "qs", C["qs"])

        sch.check()
        build.last_sched = sch

        block = ctx.enter_context(nc.Block())

        @block.sync
        def _(eng):
            sch.replay("sync", eng, sems)

        @block.vector
        def _(eng):
            sch.replay("vector", eng, sems)

        @block.scalar
        def _(eng):
            sch.replay("scalar", eng, sems)

        @block.tensor
        def _(eng):
            sch.replay("tensor", eng, sems)

        @block.gpsimd
        def _(eng):
            sch.replay("gpsimd", eng, sems)

    return nc


def kernel(**inputs):
    import ml_dtypes
    x = np.ascontiguousarray(np.asarray(inputs["L0_attn_query"], np.float32))
    Wqkv = np.ascontiguousarray(np.asarray(inputs["L0_attn_Wqkv"], np.float32))
    Wproj = np.ascontiguousarray(np.asarray(inputs["L0_attn_Wproj"], np.float32))
    bproj = np.ascontiguousarray(
        np.asarray(inputs["L0_attn_bproj"], np.float32).reshape(1, D))
    su_ug = np.ascontiguousarray(
        np.asarray(inputs["L0_ffn_SU_ug"], np.float32).reshape(1, D))
    su_dp = np.ascontiguousarray(
        np.asarray(inputs["L0_ffn_SU_dp"], np.float32).reshape(1, DI))
    ws_ug = np.ascontiguousarray(
        np.asarray(inputs["L0_ffn_Wscale_ug"], np.float32).reshape(1, 2 * DI))
    ws_dp = np.ascontiguousarray(
        np.asarray(inputs["L0_ffn_Wscale_dp"], np.float32).reshape(1, D))
    W_up = np.ascontiguousarray(np.asarray(inputs["L0_ffn_W_up"], np.float32))
    W_gate = np.ascontiguousarray(np.asarray(inputs["L0_ffn_W_gate"], np.float32))
    W_down = np.ascontiguousarray(np.asarray(inputs["L0_ffn_W_down"], np.float32))

    h128 = _hadamard(128).astype(ml_dtypes.bfloat16)
    idb = np.eye(128, dtype=ml_dtypes.bfloat16)
    idf = np.eye(128, dtype=np.float32)
    onesb = np.ones((128, 128), dtype=ml_dtypes.bfloat16)

    in_maps = []
    for c in range(NCORES):
        in_maps.append({
            "x_loc": np.ascontiguousarray(x[:, c * SC:(c + 1) * SC, :]),
            "wq_sl": np.ascontiguousarray(Wqkv[768 * c: 768 * (c + 1), :]),
            "wp_sl": np.ascontiguousarray(Wproj[256 * c: 256 * (c + 1), :]),
            "up_sl": np.ascontiguousarray(W_up[1024 * c: 1024 * (c + 1), :]),
            "gt_sl": np.ascontiguousarray(W_gate[1024 * c: 1024 * (c + 1), :]),
            "dn_sl": np.ascontiguousarray(W_down[256 * c: 256 * (c + 1), :]),
            "v_suug": su_ug, "v_sudp": su_dp, "v_wsug": ws_ug,
            "v_wsdp": ws_dp, "v_bpr": bproj,
            "c_h128": h128, "c_idb": idb, "c_idf": idf, "c_onesb": onesb,
        })

    nc = build()
    res = run_bass_kernel_spmd(nc, in_maps, core_ids=list(range(NCORES)))
    out = np.empty((B, S, D), np.float32)
    for c in range(NCORES):
        out[:, c * SC:(c + 1) * SC, :] = res.results[c]["out_loc"]
    return out


if __name__ == "__main__":
    import jax
    jax.config.update("jax_default_device", jax.devices("cpu")[0])
    import reference as refmod
    inp = refmod.setup_inputs(0)
    out = kernel(**{k: np.asarray(v) for k, v in inp.items()})
    ref = np.asarray(refmod.reference(**inp))
    d = out - ref
    rvr = float((d ** 2).mean() / (ref.astype(np.float64) ** 2).mean())
    print("rvr", rvr, "rel", float(np.linalg.norm(d) / np.linalg.norm(ref)))
